# version 53
# baseline (speedup 1.0000x reference)
import numpy as np
import ml_dtypes
import concourse.bass as bass
import concourse.mybir as mybir
from concourse.bass_utils import run_bass_kernel_spmd
from contextlib import ExitStack

F32 = mybir.dt.float32
BF16 = mybir.dt.bfloat16
AF = mybir.ActivationFunctionType
ALU = mybir.AluOpType
X = mybir.AxisListType.X
BF = ml_dtypes.bfloat16

T, D, DS = 4096, 1024, 64
C, HH = 128, 64
NT = T // C
SEG = 4
NSEG = NT // SEG
PACKW = 64 * 3 + 8 + 128 + 1024
NXB, NPX, NXN, NOUT, NASB = 13, 8, 8, 3, 4
EPS = 1e-5


def build_nc():
    nc = bass.Bass("TRN2", target_bir_lowering=False,
                   detect_race_conditions=False)
    dt_in = dict(kind="ExternalInput")
    x_d = nc.dram_tensor("x", [T, D], F32, **dt_in).ap()
    wcat_d = nc.dram_tensor("wcatT", [D, PACKW], F32, **dt_in).ap()
    wot_d = nc.dram_tensor("woT", [65, D], F32, **dt_in).ap()
    vf_d = nc.dram_tensor("vfT", [128, D], F32, **dt_in).ap()
    bias_d = nc.dram_tensor("biascat", [1, 328], F32, **dt_in).ap()
    dec_d = nc.dram_tensor("decay2", [1, 64], F32, **dt_in).ap()
    mtri_d = nc.dram_tensor("mtri", [128, 128], BF16, **dt_in).ap()
    id_d = nc.dram_tensor("ident", [128, 128], BF16, **dt_in).ap()
    g_d = [nc.dram_tensor(f"gv{i}", [1, 128], F32, **dt_in).ap()
           for i in range(4)]
    xns_d = nc.dram_tensor("xnscr", [T, D], BF16, kind="Internal").ap()
    out_d = nc.dram_tensor("out", [T, D], F32, kind="ExternalOutput").ap()

    ctx = ExitStack()
    sb = lambda n, s, d: ctx.enter_context(nc.sbuf_tensor(n, s, d))
    ps = lambda n, s, d: ctx.enter_context(nc.psum_tensor(n, s, d))
    sem = lambda n: ctx.enter_context(nc.semaphore(n))

    wcat = [sb(f"wcat{c}", [128, PACKW], BF16) for c in range(8)]
    wot = sb("wot", [65, D], BF16)
    vf_st = sb("vf_st", [128, D], F32)
    vf = sb("vf", [128, D], BF16)
    biasbf = sb("biasbf", [1, 328], BF16)
    mtri = sb("mtri_s", [128, 128], BF16)
    ident = sb("ident_s", [128, 128], BF16)
    gv = [sb(f"gv_s{i}", [1, 128], F32) for i in range(4)]
    onesf = sb("onesf", [1, 128], F32)
    onesbf = sb("onesbf", [1, 128], BF16)
    dec = sb("dec", [1, 64], F32)
    dsig = sb("dsig", [1, 64], F32)
    lnd = sb("lnd", [1, 64], F32)
    lnd64 = sb("lnd64", [1, 64], F32)
    lnd128 = sb("lnd128", [1, 64], F32)
    ptri = sb("ptri", [128, 64], F32)
    poff = sb("poff", [64, 64], F32)
    pdp = sb("pdp", [128, 64], F32)
    rpat = sb("rpat", [64, 128], F32)
    dc64m = sb("dc64m", [64, 64], F32)
    dc128m = sb("dc128m", [64, 64], F32)
    cst = sb("cst", [128, 4], F32)

    xb = [sb(f"xb{i}", [128, D], F32) for i in range(NXB)]
    xpxn = [sb(f"xpxn{i}", [128, D], F32) for i in range(NPX)]
    xnbf = [sb(f"xnbf{i}", [128, D], BF16) for i in range(NXN)]
    xntbig = [sb(f"xntb{s}", [128, 8 * SEG * 128], BF16) for s in range(3)]
    xnt = [[xntbig[s][:, SEG * 128 * c:SEG * 128 * (c + 1)] for c in range(8)]
           for s in range(3)]
    asb = [sb(f"asb{i}", [128, 328], BF16) for i in range(NASB)]
    outb = [sb(f"outb{i}", [128, D], F32) for i in range(NOUT)]
    sqs = sb("sqs", [128, D], BF16)
    cps = sb("cps", [128, D], BF16)

    s_sum = sb("s_sum", [128, NT], F32)
    s_sq = sb("s_sq", [128, NT], F32)
    s_mean = sb("s_mean", [128, NT], F32)
    s_msq = sb("s_msq", [128, NT], F32)
    s_vq = sb("s_vq", [128, NT], F32)
    s_var = sb("s_var", [128, NT], F32)
    s_rstd = sb("s_rstd", [128, NT], F32)
    s_nmr = sb("s_nmr", [128, NT], F32)
    s_opr = sb("s_opr", [128, NT], F32)

    qcm = [sb(f"qcm{i}", [64, 128], BF16) for i in range(2)]
    kcm = [sb(f"kcm{i}", [64, 128], BF16) for i in range(2)]
    atri = [sb(f"atri{i}", [128, 128], BF16) for i in range(2)]
    aoff = [sb(f"aoff{i}", [64, 64], BF16) for i in range(2)]
    vtri = [sb(f"vtri{i}", [128, 64], BF16) for i in range(2)]
    voff = [sb(f"voff{i}", [64, 64], BF16) for i in range(2)]
    vdt = [sb(f"vdt{i}", [128, 64], BF16) for i in range(2)]
    scext = [sb(f"scext{i}", [65, 128], BF16) for i in range(2)]
    st = sb("st", [64, 64], F32)
    sttmp = sb("sttmp", [64, 64], F32)
    st64 = sb("st64", [64, 64], F32)
    stbf = sb("stbf", [64, 64], BF16)
    st64bf = sb("st64bf", [64, 64], BF16)
    gexp = [sb(f"gexp{i}", [128, 8], F32) for i in range(2)]
    gln = [sb(f"gln{i}", [128, 1], F32) for i in range(2)]
    gsum = [sb(f"gsum{i}", [128, 1], F32) for i in range(2)]
    grec = [sb(f"grec{i}", [128, 1], F32) for i in range(2)]
    gttm = [sb(f"gttm{i}", [128, 128], BF16) for i in range(2)]
    gtcm = [sb(f"gtcm{i}", [128, 128], BF16) for i in range(2)]

    psA = ps("psA", [128, 328], F32)
    psB = [ps(f"psB{i}", [128, 1024], F32) for i in range(3)]
    ps6 = ps("ps6", [128, 512], F32)
    ps7 = ps6
    at_ps = ps6[:, 0:128]
    o_ps = ps6[0:64, 128:256]
    gt_ps = ps6[0:64, 256:320]
    trq_ps = ps6[0:64, 320:384].bitcast(BF16)
    trk_ps = ps6[0:64, 384:448].bitcast(BF16)
    gttr_ps = ps6[:, 448:512].bitcast(BF16)

    m_x, m_wl, m_wl2 = sem("m_x"), sem("m_wl"), sem("m_wl2")
    m_xnst, m_xnt, m_out = sem("m_xnst"), sem("m_xnt"), sem("m_out")
    m_xnt2 = sem("m_xnt2")
    c_sln, c_sln2 = sem("c_sln"), sem("c_sln2")
    c_prepV, c_pmm, c_prepS = sem("c_prepV"), sem("c_pmm"), sem("c_prepS")
    c_sf, c_fv = sem("c_sf"), sem("c_fv")
    c_xnb, c_xpxn = sem("c_xnb"), sem("c_xpxn")
    c_xnb2 = sem("c_xnb2")
    c_packA, c_packB = sem("c_packA"), sem("c_packB")
    c_Aev, c_trqk, c_qkev = sem("c_Aev"), sem("c_trqk"), sem("c_qkev")
    c_At, c_scA, c_G = sem("c_At"), sem("c_scA"), sem("c_G")
    c_oi, c_Gt, c_st = sem("c_oi"), sem("c_Gt"), sem("c_st")
    c_rs, c_ge = sem("c_rs"), sem("c_ge")
    c_gt, c_gttrT, c_gtev = sem("c_gt"), sem("c_gttrT"), sem("c_gtev")
    c_Bev = sem("c_Bev")
    c_pt0, c_xnt0 = sem("c_pt0"), sem("c_xnt0")

    NCONST = 10

    seg_of = lambda u: u // SEG

    with nc.Block() as block:

        @block.sync
        def _(eng):
            for u in range(NXB):
                eng.dma_start(out=xb[u][:],
                              in_=x_d[128 * u:128 * (u + 1), :]
                              ).then_inc(m_x, 16)

            def store_xn(u):
                eng.wait_ge(c_xnb, u + 1)
                eng.dma_start(out=xns_d[128 * u:128 * (u + 1), :],
                              in_=xnbf[u % NXN][:]).then_inc(m_xnst, 16)

            def transposes(s, c0, c1):
                eng.wait_ge(m_xnst, 16 * (SEG * (s + 1)))
                if s >= 2:
                    eng.wait_ge(m_xnt, 16 * 8 * (s - 1))
                if s >= 3:
                    eng.wait_ge(c_packA, SEG * (s - 2))
                t0 = s * SEG * 128
                for c in range(c0, c1):
                    eng.dma_start(
                        out=xnt[s % 3][c],
                        in_=xns_d[t0:t0 + SEG * 128, 128 * c:128 * (c + 1)],
                        transpose=True).then_inc(m_xnt, 16)

            for u in range(7):
                store_xn(u)
            for t in range(NT + 1):
                u = t + 7
                if 7 <= u < NT:
                    store_xn(u)
                if t % SEG == 0:
                    s = t // SEG + 1
                    if 1 <= s < NSEG:
                        transposes(s, 0, 8)

        @block.gpsimd
        def _(eng):
            def ldx(u):
                eng.dma_start(out=xb[u % NXB][:],
                              in_=x_d[128 * u:128 * (u + 1), :]
                              ).then_inc(m_x, 16)
            for i in range(4):
                eng.dma_start(out=gv[i][:], in_=g_d[i]).then_inc(m_wl2, 16)
            eng.dma_start(out=dec[:], in_=dec_d).then_inc(m_wl2, 16)
            eng.dma_start(out=mtri[:], in_=mtri_d).then_inc(m_wl2, 16)
            eng.dma_start(out=ident[:], in_=id_d).then_inc(m_wl2, 16)
            eng.dma_start(out=wot[:], in_=wot_d).then_inc(m_wl2, 16)
            eng.dma_start(out=biasbf[:], in_=bias_d).then_inc(m_wl2, 16)
            eng.dma_start(out=vf_st[:], in_=vf_d).then_inc(m_wl2, 16)
            for c in range(8):
                eng.dma_start(out=wcat[c][:],
                              in_=wcat_d[128 * c:128 * (c + 1), :]
                              ).then_inc(m_wl, 16)
            for t in range(NT + 1):
                u = t + NXB
                if u < NT:
                    eng.wait_ge(c_xnb2, u - NXB + 1)
                    eng.wait_ge(c_xpxn, u - NXB + 1)
                    ldx(u)
                if t >= 1:
                    tt = t - 1
                    p = tt % 2
                    a = asb[tt % NASB]
                    if tt == 0:
                        eng.wait_ge(c_prepS, 1)
                    if tt >= 2:
                        eng.wait_ge(c_Gt, tt - 1)
                    eng.wait_ge(c_ge, tt + 1)
                    eng.tensor_tensor(vtri[p][:], a[:, 128:192], ptri[:],
                                      ALU.mult)
                    eng.tensor_tensor(voff[p][:], a[0:64, 128:192], poff[:],
                                      ALU.mult)
                    eng.tensor_tensor(vdt[p][:], a[:, 128:192], pdp[:],
                                      ALU.mult).then_inc(c_G, 1)
                if t >= 2:
                    ss = t - 2
                    eng.wait_ge(c_Bev, ss + 1)
                    eng.dma_start(out=out_d[128 * ss:128 * (ss + 1), :],
                                  in_=outb[ss % NOUT][:]).then_inc(m_out, 16)
            eng.wait_ge(c_Bev, NT)
            eng.dma_start(out=out_d[128 * (NT - 1):128 * NT, :],
                          in_=outb[(NT - 1) % NOUT][:]).then_inc(m_out, 16)
            eng.wait_ge(m_out, 16 * NT)

        @block.vector
        def _(eng):
            eng.memset(cst[:, 0:1], EPS)
            eng.memset(cst[:, 1:2], 1.0)
            eng.memset(cst[:, 2:3], -1.0)
            eng.memset(cst[:, 3:4], 0.75)
            eng.memset(onesf[:], 1.0)
            eng.memset(onesbf[:], 1.0)
            eng.memset(scext[0][64:65, :], 1.0)
            eng.memset(scext[1][64:65, :], 1.0)
            eng.memset(st[:], 0.0)
            eng.memset(st64[:], 0.0)
            eng.memset(stbf[:], 0.0)
            eng.memset(st64bf[:], 0.0).then_inc(c_st, 1)

            def finalizeV(s):
                cs = slice(SEG * s, SEG * (s + 1))
                eng.wait_ge(c_sf, 2 * s + 1)
                eng.tensor_sub(s_var[:, cs], s_vq[:, cs],
                               s_msq[:, cs])
                eng.drain(fusable=True).then_inc(c_fv, 1)
                eng.wait_ge(c_sf, 2 * s + 2)
                eng.scalar_tensor_tensor(
                    s_nmr[:, cs], s_mean[:, cs], cst[:, 2:3], s_rstd[:, cs],
                    ALU.mult, ALU.mult)
                eng.tensor_scalar_add(s_opr[:, cs], s_rstd[:, cs],
                                      cst[:, 1:2])
                eng.drain(fusable=True).then_inc(c_fv, 1)

            def do_xpxn(u):
                eng.wait_ge(m_x, 16 * (u + 1))
                eng.tensor_scalar(
                    xpxn[u % NPX][:], xb[u % NXB][:],
                    s_opr[:, u:u + 1], s_nmr[:, u:u + 1],
                    ALU.mult, ALU.add).then_inc(c_xpxn, 1)

            finalizeV(0)
            for u in range(4):
                eng.wait_ge(c_pt0, u + 1)
                src = psB[2][:, 512 * (u % 2):512 * (u % 2) + 512] \
                    .bitcast(BF16).rearrange("p (c r) -> p c r", r=128)
                dst = xntbig[0][:, :].rearrange(
                    "p (c r) -> p c r", r=SEG * 128)[:, :, 128 * u:128 * (u + 1)]
                if u < 3:
                    eng.tensor_copy(dst, src).then_inc(c_xnt0, 1)
                else:
                    eng.tensor_copy(dst, src)
                    eng.drain(fusable=True).then_inc(c_xnt0, 1)
            eng.wait_ge(c_sln, 1)
            eng.tensor_scalar_min(lnd[:], lnd[:], cst[0:1, 3:4])
            eng.drain(fusable=True).then_inc(c_prepV, 1)
            finalizeV(1)
            for u in range(8):
                do_xpxn(u)

            for t in range(NT + 1):
                if t >= 2:
                    ee = t - 2
                    if ee >= NOUT:
                        eng.wait_ge(m_out, 16 * (ee - NOUT + 1))
                    eng.wait_ge(c_packB, ee + 1)
                    eng.tensor_tensor(outb[ee % NOUT][:], psB[ee % 3][:],
                                      xpxn[ee % NPX][:],
                                      ALU.add)
                    eng.drain(fusable=True).then_inc(c_Bev, 1)
                if t >= 1:
                    tt = t - 1
                    p = tt % 2
                    if tt == 0:
                        eng.wait_ge(m_wl2, 16 * NCONST)
                        eng.wait_ge(c_prepS, 1)
                    if tt >= 2:
                        eng.wait_ge(c_oi, tt - 1)
                    eng.wait_ge(c_trqk, tt + 1)
                    eng.tensor_copy(qcm[p][:], trq_ps)
                    eng.tensor_copy(kcm[p][:], trk_ps).then_inc(c_qkev, 1)
                    if tt >= 2:
                        eng.wait_ge(c_Gt, tt - 1)
                    eng.wait_ge(c_At, tt + 1)
                    eng.tensor_tensor(atri[p][:], at_ps, mtri[:], ALU.mult)
                    eng.tensor_copy(aoff[p][:],
                                    at_ps[0:64, 64:128]).then_inc(c_scA, 1)
                if t % SEG == 0:
                    s = t // SEG + 2
                    if 2 <= s < NSEG:
                        finalizeV(s)
                if t >= 1:
                    tt = t - 1
                    p = tt % 2
                    eng.wait_ge(c_ge, tt + 1)
                    if tt >= 2:
                        eng.wait_ge(c_gttrT, tt - 1)
                    a_t = asb[tt % NASB]
                    eng.scalar_tensor_tensor(
                        gttm[p][:, :].rearrange("p (e r) -> p e r", e=8),
                        a_t[:, 200:328].rearrange("p (e r) -> p e r", e=8),
                        grec[p][:, 0:1],
                        gexp[p][:, :].unsqueeze(2).broadcast_to((128, 8, 16)),
                        ALU.mult, ALU.mult).then_inc(c_gt, 1)
                if t >= 1:
                    tt = t - 1
                    p = tt % 2
                    if tt >= 2:
                        eng.wait_ge(c_packB, tt - 1)
                    eng.wait_ge(c_oi, tt + 1)
                    eng.tensor_tensor(scext[p][0:64, :], o_ps, rpat[:],
                                      ALU.mult).then_inc(c_rs, 1)
                    if tt >= 2:
                        eng.wait_ge(c_packB, tt - 1)
                    eng.wait_ge(c_gttrT, tt + 1)
                    eng.tensor_copy(gtcm[p][:], gttr_ps).then_inc(c_gtev, 1)
                    eng.tensor_tensor(sttmp[:], st[:], dc128m[:], ALU.mult)
                    eng.wait_ge(c_Gt, tt + 1)
                    eng.tensor_tensor(st[:], sttmp[:], gt_ps, ALU.add)
                    eng.tensor_tensor(st64[:], st[:], dc64m[:], ALU.mult)
                    eng.tensor_copy(stbf[:], st[:])
                    eng.tensor_copy(st64bf[:], st64[:]).then_inc(c_st, 1)
                u = t + 4
                if 8 <= u < NT:
                    do_xpxn(u)
            eng.wait_ge(m_out, 16 * (NT - 1 - NOUT + 1))
            eng.wait_ge(c_packB, NT)
            eng.tensor_tensor(outb[(NT - 1) % NOUT][:], psB[(NT - 1) % 3][:],
                              xpxn[(NT - 1) % NPX][:], ALU.add)
            eng.drain(fusable=True).then_inc(c_Bev, 1)

        @block.scalar
        def _(eng):

            def stats(u):
                eng.wait_ge(m_x, 16 * min(u + 2, NT))
                eng.activation(sqs[:], xb[u % NXB][:], AF.Square,
                               accum_out=s_sq[:, u:u + 1])
                eng.activation(cps[:], xb[u % NXB][:], AF.Identity,
                               accum_out=s_sum[:, u:u + 1])

            def finalizeS1(s):
                cs = slice(SEG * s, SEG * (s + 1))
                eng.activation(s_mean[:, cs], s_sum[:, cs], AF.Identity,
                               scale=1.0 / D)
                eng.activation(s_msq[:, cs], s_mean[:, cs], AF.Square)
                eng.activation(s_vq[:, cs], s_sq[:, cs], AF.Identity,
                               scale=1.0 / D)
                eng.drain(fusable=True).then_inc(c_sf, 1)

            def finalizeS2(s):
                cs = slice(SEG * s, SEG * (s + 1))
                eng.wait_ge(c_fv, 2 * s + 1)
                eng.activation(s_vq[:, cs], s_var[:, cs], AF.Ln,
                               bias=cst[:, 0:1])
                eng.activation(s_rstd[:, cs], s_vq[:, cs], AF.Exp,
                               scale=-0.5)
                eng.drain(fusable=True).then_inc(c_sf, 1)


            def do_xn(u):
                eng.wait_ge(c_fv, 2 * seg_of(u) + 2)
                if u >= NXN:
                    eng.wait_ge(m_xnst, 16 * (u - NXN + 1))
                if 8 <= u < 12:
                    eng.wait_ge(c_pt0, u - 7)
                eng.activation(xnbf[u % NXN][:], xb[u % NXB][:], AF.Identity,
                               bias=s_nmr[:, u:u + 1],
                               scale=s_rstd[:, u:u + 1]).then_inc(c_xnb2, 1)
                if u % SEG == SEG - 1:
                    eng.drain(fusable=True).then_inc(c_xnb, SEG)

            stats(0)
            stats(1)
            stats(2)
            stats(3)
            finalizeS1(0)
            finalizeS2(0)
            for u in range(4):
                do_xn(u)
            eng.wait_ge(m_wl2, 16 * NCONST)
            eng.activation(dsig[:], dec[:], AF.Exp, scale=-1.0)
            eng.activation(lnd[:], dsig[:], AF.Ln, bias=cst[0:1, 1:2])
            eng.drain(fusable=True).then_inc(c_sln, 1)
            eng.wait_ge(c_prepV, 1)
            eng.activation(lnd64[:], lnd[:], AF.Identity, scale=-64.0)
            eng.activation(lnd128[:], lnd[:], AF.Identity,
                           scale=-128.0)
            eng.drain(fusable=True).then_inc(c_sln2, 1)
            stats(4)
            stats(5)
            stats(6)
            stats(7)
            finalizeS1(1)
            finalizeS2(1)
            for u in range(4, 7):
                do_xn(u)
            stats(8)
            stats(9)
            stats(10)
            eng.wait_ge(c_pmm, 1)
            eng.activation(ptri[:], ps6[:, 0:64], AF.Exp)
            eng.activation(poff[:], ps6[0:64, 64:128], AF.Exp)
            eng.activation(pdp[:], ps6[:, 128:192], AF.Exp)
            eng.activation(rpat[:], ps6[0:64, 192:320], AF.Exp)
            eng.activation(dc64m[:], ps6[0:64, 320:384], AF.Exp)
            eng.activation(dc128m[:], ps6[0:64, 384:448], AF.Exp)
            eng.wait_ge(m_wl2, 16 * NCONST)
            eng.activation(vf[:], vf_st[:], AF.Identity, scale=1.0 / 16.0)
            eng.drain(fusable=True).then_inc(c_prepS, 1)


            for t in range(NT + 1):
                if t == 0:
                    eng.wait_ge(c_packA, 1)
                    eng.activation(asb[0][:], psA[:], AF.Identity)
                    eng.drain(fusable=True).then_inc(c_Aev, 1)
                if t >= 1:
                    tt = t - 1
                    p = tt % 2
                    if tt >= 2:
                        eng.wait_ge(c_gt, tt - 1)
                    eng.activation(gexp[p][:], asb[tt % NASB][:, 192:200],
                                   AF.Exp, accum_out=gsum[p][:])
                    eng.activation(gln[p][:], gsum[p][:], AF.Ln)
                    eng.activation(grec[p][:], gln[p][:], AF.Exp, scale=-1.0)
                    eng.drain(fusable=True).then_inc(c_ge, 1)
                u = t + 11
                if u < NT:
                    stats(u)
                if t % SEG == 0:
                    s = t // SEG + 2
                    if 2 <= s < NSEG:
                        finalizeS1(s)
                        finalizeS2(s)
                u = t + 7
                if 7 <= u < NT:
                    do_xn(u)
                if 1 <= t < NT:
                    if t >= 3:
                        eng.wait_ge(c_Gt, t - 2)
                        eng.wait_ge(c_gt, t - 2)
                    eng.wait_ge(c_packA, t + 1)
                    eng.activation(asb[t % NASB][:], psA[:],
                                   AF.Identity).then_inc(c_Aev, 1)

        @block.tensor
        def _(eng):
            eng.wait_ge(m_wl2, 16 * NCONST)
            for u in range(4):
                eng.wait_ge(c_xnb2, min(u + 2, 4))
                if u >= 2:
                    eng.wait_ge(c_xnt0, u - 1)
                half = psB[2][:, 512 * (u % 2):512 * (u % 2) + 512] \
                    .bitcast(BF16)
                for c in range(8):
                    mm = eng.transpose(half[:, 128 * c:128 * (c + 1)],
                                       xnbf[u][:, 128 * c:128 * (c + 1)],
                                       ident[:])
                    if c == 7:
                        mm.then_inc(c_pt0, 1)
            eng.wait_ge(c_sln2, 1)
            eng.matmul(ps6[:, 0:64], gv[0][:], lnd[:], start=True, stop=True)
            eng.matmul(ps6[0:64, 64:128], gv[1][0:1, 0:64], lnd[:],
                       start=True, stop=True)
            eng.matmul(ps6[:, 128:192], gv[2][:], lnd[:], start=True,
                       stop=True)
            eng.matmul(ps6[0:64, 192:320], lnd[:], gv[3][:], start=True,
                       stop=True)
            eng.matmul(ps6[0:64, 320:384], onesf[0:1, 0:64], lnd64[:],
                       start=True, stop=True)
            eng.matmul(ps6[0:64, 384:448], onesf[0:1, 0:64], lnd128[:],
                       start=True, stop=True).then_inc(c_pmm, 1)

            for t in range(NT + 1):
                if t >= 1:
                    tt = t - 1
                    p = tt % 2
                    a = asb[tt % NASB]
                    if tt == 0:
                        eng.wait_ge(m_wl2, 16 * NCONST)
                        eng.wait_ge(c_prepS, 1)
                    eng.wait_ge(c_Aev, tt + 1)
                    eng.wait_ge(c_st, tt + 1)
                    eng.transpose(trq_ps, a[:, 0:64], ident[:])
                    eng.transpose(trk_ps, a[:, 64:128],
                                  ident[:]).then_inc(c_trqk, 1)
                if t < NT:
                    s = seg_of(t)
                    pq = t % 3
                    off = (t % SEG) * 128
                    if t == 0:
                        eng.wait_ge(m_wl2, 16 * NCONST)
                    if s == 0:
                        eng.wait_ge(c_xnt0, 4)
                    else:
                        eng.wait_ge(m_xnt, 16 * 8 * s)
                    if t >= 3:
                        eng.wait_ge(c_Bev, t - 2)
                    if t == 0:
                        eng.wait_ge(m_wl, 16 * 8)
                    for c in range(8):
                        lhs = xnt[s % 3][c][:, off:off + 128]
                        if c < 7 or t == 0:
                            eng.matmul(psB[pq][:, 0:512], lhs,
                                       wcat[c][:, 328:840], start=(c == 0),
                                       stop=False)
                            eng.matmul(psB[pq][:, 512:1024], lhs,
                                       wcat[c][:, 840:1352], start=(c == 0),
                                       stop=False)
                        eng.matmul(psA[:], lhs, wcat[c][:, 0:328],
                                   start=(c == 0), stop=False)
                        if c == 2 and t >= 1:
                            eng.wait_ge(c_qkev, t)
                            eng.matmul(at_ps, kcm[(t - 1) % 2][:],
                                       qcm[(t - 1) % 2][:], start=True,
                                       stop=True).then_inc(c_At, 1)
                    eng.matmul(psA[:], onesbf[:], biasbf[:],
                               start=False, stop=True).then_inc(c_packA, 1)
                if t == NT:
                    eng.wait_ge(c_qkev, t)
                    eng.matmul(at_ps, kcm[(t - 1) % 2][:],
                               qcm[(t - 1) % 2][:], start=True,
                               stop=True).then_inc(c_At, 1)
                if t >= 1:
                    tt = t - 1
                    p = tt % 2
                    a = asb[tt % NASB]
                    p3 = tt % 3
                    eng.wait_ge(c_scA, tt + 1)
                    eng.wait_ge(c_G, tt + 1)
                    eng.matmul(o_ps[:, 0:128], vtri[p][:], atri[p][:],
                               start=True, stop=False)
                    eng.matmul(o_ps[:, 64:128], voff[p][:], aoff[p][:],
                               start=False, stop=False)
                    eng.matmul(o_ps[:, 64:128], st64bf[:],
                               qcm[p][:, 64:128], start=False, stop=False)
                    eng.matmul(o_ps[:, 0:64], stbf[:], qcm[p][:, 0:64],
                               start=False,
                               stop=True).then_inc(c_oi, 1)
                    lhs7 = xnt[seg_of(min(t, NT - 1)) % 3][7][
                        :, (t % SEG) * 128:(t % SEG) * 128 + 128]
                    if 1 <= t < NT:
                        eng.matmul(psB[t % 3][:, 0:512], lhs7,
                                   wcat[7][:, 328:840], start=False,
                                   stop=False)
                        eng.matmul(psB[t % 3][:, 512:1024], lhs7,
                                   wcat[7][:, 840:1352], start=False,
                                   stop=False)
                    eng.wait_ge(c_rs, tt + 1)
                    eng.matmul(gt_ps, a[:, 64:128], vdt[p][:], start=True,
                               stop=True).then_inc(c_Gt, 1)
                    eng.wait_ge(c_gt, tt + 1)
                    eng.transpose(gttr_ps, gttm[p][:],
                                  ident[:]).then_inc(c_gttrT, 1)
                    eng.matmul(psB[p3][:, 0:512], scext[p][0:65, :],
                               wot[0:65, 0:512], start=False, stop=False)
                    eng.matmul(psB[p3][:, 512:1024], scext[p][0:65, :],
                               wot[0:65, 512:1024], start=False, stop=False)
                    eng.wait_ge(c_gtev, tt + 1)
                    eng.matmul(psB[p3][:, 0:512], gtcm[p][:], vf[:, 0:512],
                               start=False, stop=True)
                    eng.matmul(psB[p3][:, 512:1024], gtcm[p][:],
                               vf[:, 512:1024], start=False,
                               stop=True).then_inc(c_packB, 1)

    ctx.pop_all()
    return nc


def host_layout(inputs):
    f32 = np.float32
    Wq = np.asarray(inputs["L0_attn_Wq"], f32)
    Wk = np.asarray(inputs["L0_attn_Wk"], f32)
    Wv = np.asarray(inputs["L0_attn_Wv"], f32)
    gw = np.asarray(inputs["L0_ffn_gate_w"], f32)
    U = np.asarray(inputs["L0_ffn_lora_U"], f32).reshape(128, 1024)
    Wb = np.asarray(inputs["L0_ffn_weight"], f32)
    wcatT = np.ascontiguousarray(
        np.concatenate([Wq, Wk, Wv, gw, U, Wb], axis=0).T)
    woT = np.ascontiguousarray(np.concatenate(
        [np.asarray(inputs["L0_attn_Wo"], f32).T,
         np.asarray(inputs["L0_attn_bo"], f32)[None, :]], axis=0))
    vfT = np.ascontiguousarray(
        np.asarray(inputs["L0_ffn_lora_V"], f32).transpose(0, 2, 1)
        .reshape(128, 1024))
    biascat = np.zeros((1, 328), f32)
    biascat[0, 0:64] = np.asarray(inputs["L0_attn_bq"], f32)
    biascat[0, 64:128] = np.asarray(inputs["L0_attn_bk"], f32)
    biascat[0, 128:192] = np.asarray(inputs["L0_attn_bv"], f32)
    decay2 = np.ascontiguousarray(
        np.asarray(inputs["L0_attn_decay"], f32)[None, :])
    sg, ta = np.arange(128), np.arange(128)
    mtri = ((sg[:, None] // HH == ta[None, :] // HH)
            & (sg[:, None] <= ta[None, :])).astype(f32).astype(BF)
    ident = np.eye(128, dtype=f32).astype(BF)
    gvs = [
        (sg % HH).astype(f32)[None, :],
        -np.where(sg < HH, HH - sg, 0).astype(f32)[None, :],
        -(128 - sg).astype(f32)[None, :],
        (-(sg % HH)).astype(f32)[None, :],
    ]
    return wcatT, woT, vfT, biascat, decay2, mtri, ident, gvs


def kernel(**inputs):
    x = np.ascontiguousarray(np.asarray(inputs["L0_attn_x"], np.float32))
    wcatT, woT, vfT, biascat, decay2, mtri, ident, gvs = host_layout(inputs)
    nc = build_nc()
    in_maps = []
    for b in range(8):
        im = {"x": np.ascontiguousarray(x[b]), "wcatT": wcatT, "woT": woT,
              "vfT": vfT, "biascat": biascat, "decay2": decay2,
              "mtri": mtri, "ident": ident}
        for i in range(4):
            im[f"gv{i}"] = gvs[i]
        in_maps.append(im)
    res = run_bass_kernel_spmd(nc, in_maps, core_ids=list(range(8)))
    out = np.stack([np.asarray(r["out"], np.float32) for r in res.results])
    return out


# version 54
# speedup vs baseline: 1.0073x; 1.0073x over previous
import numpy as np
import ml_dtypes
import concourse.bass as bass
import concourse.mybir as mybir
from concourse.bass_utils import run_bass_kernel_spmd
from contextlib import ExitStack

F32 = mybir.dt.float32
BF16 = mybir.dt.bfloat16
AF = mybir.ActivationFunctionType
ALU = mybir.AluOpType
X = mybir.AxisListType.X
BF = ml_dtypes.bfloat16

T, D, DS = 4096, 1024, 64
C, HH = 128, 64
NT = T // C
SEG = 4
NSEG = NT // SEG
PACKW = 64 * 3 + 8 + 128 + 1024
NXB, NPX, NXN, NOUT, NASB = 13, 8, 8, 3, 4
EPS = 1e-5


def build_nc():
    nc = bass.Bass("TRN2", target_bir_lowering=False,
                   detect_race_conditions=False)
    dt_in = dict(kind="ExternalInput")
    x_d = nc.dram_tensor("x", [T, D], F32, **dt_in).ap()
    wcat_d = nc.dram_tensor("wcatT", [D, PACKW], F32, **dt_in).ap()
    wot_d = nc.dram_tensor("woT", [65, D], F32, **dt_in).ap()
    vf_d = nc.dram_tensor("vfT", [128, D], F32, **dt_in).ap()
    bias_d = nc.dram_tensor("biascat", [1, 328], F32, **dt_in).ap()
    dec_d = nc.dram_tensor("decay2", [1, 64], F32, **dt_in).ap()
    mtri_d = nc.dram_tensor("mtri", [128, 128], BF16, **dt_in).ap()
    id_d = nc.dram_tensor("ident", [128, 128], BF16, **dt_in).ap()
    g_d = [nc.dram_tensor(f"gv{i}", [1, 128], F32, **dt_in).ap()
           for i in range(4)]
    xns_d = nc.dram_tensor("xnscr", [T, D], BF16, kind="Internal").ap()
    out_d = nc.dram_tensor("out", [T, D], F32, kind="ExternalOutput").ap()

    ctx = ExitStack()
    sb = lambda n, s, d: ctx.enter_context(nc.sbuf_tensor(n, s, d))
    ps = lambda n, s, d: ctx.enter_context(nc.psum_tensor(n, s, d))
    sem = lambda n: ctx.enter_context(nc.semaphore(n))

    wcat = [sb(f"wcat{c}", [128, PACKW], BF16) for c in range(8)]
    wot = sb("wot", [65, D], BF16)
    vf_st = sb("vf_st", [128, D], F32)
    vf = sb("vf", [128, D], BF16)
    biasbf = sb("biasbf", [1, 328], BF16)
    mtri = sb("mtri_s", [128, 128], BF16)
    ident = sb("ident_s", [128, 128], BF16)
    gv = [sb(f"gv_s{i}", [1, 128], F32) for i in range(4)]
    onesf = sb("onesf", [1, 128], F32)
    onesbf = sb("onesbf", [1, 128], BF16)
    dec = sb("dec", [1, 64], F32)
    dsig = sb("dsig", [1, 64], F32)
    lnd = sb("lnd", [1, 64], F32)
    lnd64 = sb("lnd64", [1, 64], F32)
    lnd128 = sb("lnd128", [1, 64], F32)
    ptri = sb("ptri", [128, 64], F32)
    poff = sb("poff", [64, 64], F32)
    pdp = sb("pdp", [128, 64], F32)
    rpat = sb("rpat", [64, 128], F32)
    dc64m = sb("dc64m", [64, 64], F32)
    dc128m = sb("dc128m", [64, 64], F32)
    cst = sb("cst", [128, 4], F32)

    xb = [sb(f"xb{i}", [128, D], F32) for i in range(NXB)]
    xpxn = [sb(f"xpxn{i}", [128, D], F32) for i in range(NPX)]
    xnbf = [sb(f"xnbf{i}", [128, D], BF16) for i in range(NXN)]
    xntbig = [sb(f"xntb{s}", [128, 8 * SEG * 128], BF16) for s in range(3)]
    xnt = [[xntbig[s][:, SEG * 128 * c:SEG * 128 * (c + 1)] for c in range(8)]
           for s in range(3)]
    asb = [sb(f"asb{i}", [128, 328], BF16) for i in range(NASB)]
    outb = [sb(f"outb{i}", [128, D], F32) for i in range(NOUT)]
    sqs = sb("sqs", [128, D], BF16)
    cps = sb("cps", [128, D], BF16)

    s_sum = sb("s_sum", [128, NT], F32)
    s_sq = sb("s_sq", [128, NT], F32)
    s_mean = sb("s_mean", [128, NT], F32)
    s_msq = sb("s_msq", [128, NT], F32)
    s_vq = sb("s_vq", [128, NT], F32)
    s_var = sb("s_var", [128, NT], F32)
    s_rstd = sb("s_rstd", [128, NT], F32)
    s_nmr = sb("s_nmr", [128, NT], F32)
    s_opr = sb("s_opr", [128, NT], F32)

    qcm = [sb(f"qcm{i}", [64, 128], BF16) for i in range(2)]
    kcm = [sb(f"kcm{i}", [64, 128], BF16) for i in range(2)]
    atri = [sb(f"atri{i}", [128, 128], BF16) for i in range(2)]
    aoff = [sb(f"aoff{i}", [64, 64], BF16) for i in range(2)]
    vtri = [sb(f"vtri{i}", [128, 64], BF16) for i in range(2)]
    voff = [sb(f"voff{i}", [64, 64], BF16) for i in range(2)]
    vdt = [sb(f"vdt{i}", [128, 64], BF16) for i in range(2)]
    scext = [sb(f"scext{i}", [65, 128], BF16) for i in range(2)]
    st = sb("st", [64, 64], F32)
    sttmp = sb("sttmp", [64, 64], F32)
    st64 = sb("st64", [64, 64], F32)
    stbf = sb("stbf", [64, 64], BF16)
    st64bf = sb("st64bf", [64, 64], BF16)
    gexp = [sb(f"gexp{i}", [128, 8], F32) for i in range(2)]
    gln = [sb(f"gln{i}", [128, 1], F32) for i in range(2)]
    gsum = [sb(f"gsum{i}", [128, 1], F32) for i in range(2)]
    grec = [sb(f"grec{i}", [128, 1], F32) for i in range(2)]
    gttm = [sb(f"gttm{i}", [128, 128], BF16) for i in range(2)]
    gtcm = [sb(f"gtcm{i}", [128, 128], BF16) for i in range(2)]

    psA = ps("psA", [128, 328], F32)
    psB = [ps(f"psB{i}", [128, 1024], F32) for i in range(3)]
    ps6 = ps("ps6", [128, 512], F32)
    ps7 = ps6
    at_ps = ps6[:, 0:128]
    o_ps = ps6[0:64, 128:256]
    gt_ps = ps6[0:64, 256:320]
    trq_ps = ps6[0:64, 320:384].bitcast(BF16)
    trk_ps = ps6[0:64, 384:448].bitcast(BF16)
    gttr_ps = ps6[:, 448:512].bitcast(BF16)

    m_x, m_wl, m_wl2 = sem("m_x"), sem("m_wl"), sem("m_wl2")
    m_x0 = sem("m_x0")
    m_xnst, m_xnt, m_out = sem("m_xnst"), sem("m_xnt"), sem("m_out")
    m_xnt2 = sem("m_xnt2")
    c_sln, c_sln2 = sem("c_sln"), sem("c_sln2")
    c_prepV, c_pmm, c_prepS = sem("c_prepV"), sem("c_pmm"), sem("c_prepS")
    c_sf, c_fv = sem("c_sf"), sem("c_fv")
    c_xnb, c_xpxn = sem("c_xnb"), sem("c_xpxn")
    c_xnb2 = sem("c_xnb2")
    c_packA, c_packB = sem("c_packA"), sem("c_packB")
    c_Aev, c_trqk, c_qkev = sem("c_Aev"), sem("c_trqk"), sem("c_qkev")
    c_At, c_scA, c_G = sem("c_At"), sem("c_scA"), sem("c_G")
    c_oi, c_Gt, c_st = sem("c_oi"), sem("c_Gt"), sem("c_st")
    c_rs, c_ge = sem("c_rs"), sem("c_ge")
    c_gt, c_gttrT, c_gtev = sem("c_gt"), sem("c_gttrT"), sem("c_gtev")
    c_Bev = sem("c_Bev")
    c_pt0, c_xnt0 = sem("c_pt0"), sem("c_xnt0")

    NCONST = 10

    seg_of = lambda u: u // SEG

    with nc.Block() as block:

        @block.sync
        def _(eng):
            for u in range(NXB):
                eng.dma_start(out=xb[u][:],
                              in_=x_d[128 * u:128 * (u + 1), :]
                              ).then_inc(m_x0, 16)

            def store_xn(u):
                eng.wait_ge(c_xnb, u + 1)
                eng.dma_start(out=xns_d[128 * u:128 * (u + 1), :],
                              in_=xnbf[u % NXN][:]).then_inc(m_xnst, 16)

            def transposes(s, c0, c1):
                eng.wait_ge(m_xnst, 16 * (SEG * (s + 1)))
                if s >= 2:
                    eng.wait_ge(m_xnt, 16 * 8 * (s - 1))
                if s >= 3:
                    eng.wait_ge(c_packA, SEG * (s - 2))
                t0 = s * SEG * 128
                for c in range(c0, c1):
                    eng.dma_start(
                        out=xnt[s % 3][c],
                        in_=xns_d[t0:t0 + SEG * 128, 128 * c:128 * (c + 1)],
                        transpose=True).then_inc(m_xnt, 16)

            for u in range(7):
                store_xn(u)
            for t in range(NT + 1):
                u = t + 7
                if 7 <= u < NT:
                    store_xn(u)
                if t % SEG == 0:
                    s = t // SEG + 1
                    if 1 <= s < NSEG:
                        transposes(s, 0, 8)

        @block.gpsimd
        def _(eng):
            def ldx(u):
                eng.dma_start(out=xb[u % NXB][:],
                              in_=x_d[128 * u:128 * (u + 1), :]
                              ).then_inc(m_x, 16)
            for i in range(4):
                eng.dma_start(out=gv[i][:], in_=g_d[i]).then_inc(m_wl2, 16)
            eng.dma_start(out=dec[:], in_=dec_d).then_inc(m_wl2, 16)
            eng.dma_start(out=mtri[:], in_=mtri_d).then_inc(m_wl2, 16)
            eng.dma_start(out=ident[:], in_=id_d).then_inc(m_wl2, 16)
            eng.dma_start(out=wot[:], in_=wot_d).then_inc(m_wl2, 16)
            eng.dma_start(out=biasbf[:], in_=bias_d).then_inc(m_wl2, 16)
            eng.dma_start(out=vf_st[:], in_=vf_d).then_inc(m_wl2, 16)
            for c in range(8):
                eng.dma_start(out=wcat[c][:],
                              in_=wcat_d[128 * c:128 * (c + 1), :]
                              ).then_inc(m_wl, 16)
            for t in range(NT + 1):
                u = t + NXB
                if u < NT:
                    eng.wait_ge(c_xnb2, u - NXB + 1)
                    eng.wait_ge(c_xpxn, u - NXB + 1)
                    ldx(u)
                if t >= 1:
                    tt = t - 1
                    p = tt % 2
                    a = asb[tt % NASB]
                    if tt == 0:
                        eng.wait_ge(c_prepS, 1)
                    if tt >= 2:
                        eng.wait_ge(c_Gt, tt - 1)
                    eng.wait_ge(c_ge, tt + 1)
                    eng.tensor_tensor(vtri[p][:], a[:, 128:192], ptri[:],
                                      ALU.mult)
                    eng.tensor_tensor(voff[p][:], a[0:64, 128:192], poff[:],
                                      ALU.mult)
                    eng.tensor_tensor(vdt[p][:], a[:, 128:192], pdp[:],
                                      ALU.mult).then_inc(c_G, 1)
                if t >= 2:
                    ss = t - 2
                    eng.wait_ge(c_Bev, ss + 1)
                    eng.dma_start(out=out_d[128 * ss:128 * (ss + 1), :],
                                  in_=outb[ss % NOUT][:]).then_inc(m_out, 16)
            eng.wait_ge(c_Bev, NT)
            eng.dma_start(out=out_d[128 * (NT - 1):128 * NT, :],
                          in_=outb[(NT - 1) % NOUT][:]).then_inc(m_out, 16)
            eng.wait_ge(m_out, 16 * NT)

        @block.vector
        def _(eng):
            eng.memset(cst[:, 0:1], EPS)
            eng.memset(cst[:, 1:2], 1.0)
            eng.memset(cst[:, 2:3], -1.0)
            eng.memset(cst[:, 3:4], 0.75)
            eng.memset(onesf[:], 1.0)
            eng.memset(onesbf[:], 1.0)
            eng.memset(scext[0][64:65, :], 1.0)
            eng.memset(scext[1][64:65, :], 1.0)
            eng.memset(st[:], 0.0)
            eng.memset(st64[:], 0.0)
            eng.memset(stbf[:], 0.0)
            eng.memset(st64bf[:], 0.0).then_inc(c_st, 1)

            def finalizeV(s):
                cs = slice(SEG * s, SEG * (s + 1))
                eng.wait_ge(c_sf, 2 * s + 1)
                eng.tensor_sub(s_var[:, cs], s_vq[:, cs],
                               s_msq[:, cs])
                eng.drain(fusable=True).then_inc(c_fv, 1)
                eng.wait_ge(c_sf, 2 * s + 2)
                eng.scalar_tensor_tensor(
                    s_nmr[:, cs], s_mean[:, cs], cst[:, 2:3], s_rstd[:, cs],
                    ALU.mult, ALU.mult)
                eng.tensor_scalar_add(s_opr[:, cs], s_rstd[:, cs],
                                      cst[:, 1:2])
                eng.drain(fusable=True).then_inc(c_fv, 1)

            def do_xpxn(u):
                if u + 1 <= NXB:
                    eng.wait_ge(m_x0, 16 * (u + 1))
                else:
                    eng.wait_ge(m_x, 16 * (u + 1 - NXB))
                eng.tensor_scalar(
                    xpxn[u % NPX][:], xb[u % NXB][:],
                    s_opr[:, u:u + 1], s_nmr[:, u:u + 1],
                    ALU.mult, ALU.add).then_inc(c_xpxn, 1)

            finalizeV(0)
            for u in range(4):
                eng.wait_ge(c_pt0, u + 1)
                src = psB[2][:, 512 * (u % 2):512 * (u % 2) + 512] \
                    .bitcast(BF16).rearrange("p (c r) -> p c r", r=128)
                dst = xntbig[0][:, :].rearrange(
                    "p (c r) -> p c r", r=SEG * 128)[:, :, 128 * u:128 * (u + 1)]
                if u < 3:
                    eng.tensor_copy(dst, src).then_inc(c_xnt0, 1)
                else:
                    eng.tensor_copy(dst, src)
                    eng.drain(fusable=True).then_inc(c_xnt0, 1)
            eng.wait_ge(c_sln, 1)
            eng.tensor_scalar_min(lnd[:], lnd[:], cst[0:1, 3:4])
            eng.drain(fusable=True).then_inc(c_prepV, 1)
            finalizeV(1)
            for u in range(8):
                do_xpxn(u)

            for t in range(NT + 1):
                if t >= 2:
                    ee = t - 2
                    if ee >= NOUT:
                        eng.wait_ge(m_out, 16 * (ee - NOUT + 1))
                    eng.wait_ge(c_packB, ee + 1)
                    eng.tensor_tensor(outb[ee % NOUT][:], psB[ee % 3][:],
                                      xpxn[ee % NPX][:],
                                      ALU.add)
                    eng.drain(fusable=True).then_inc(c_Bev, 1)
                if t >= 1:
                    tt = t - 1
                    p = tt % 2
                    if tt == 0:
                        eng.wait_ge(m_wl2, 16 * NCONST)
                        eng.wait_ge(c_prepS, 1)
                    if tt >= 2:
                        eng.wait_ge(c_oi, tt - 1)
                    eng.wait_ge(c_trqk, tt + 1)
                    eng.tensor_copy(qcm[p][:], trq_ps)
                    eng.tensor_copy(kcm[p][:], trk_ps).then_inc(c_qkev, 1)
                    if tt >= 2:
                        eng.wait_ge(c_Gt, tt - 1)
                    eng.wait_ge(c_At, tt + 1)
                    eng.tensor_tensor(atri[p][:], at_ps, mtri[:], ALU.mult)
                    eng.tensor_copy(aoff[p][:],
                                    at_ps[0:64, 64:128]).then_inc(c_scA, 1)
                if t % SEG == 0:
                    s = t // SEG + 2
                    if 2 <= s < NSEG:
                        finalizeV(s)
                if t >= 1:
                    tt = t - 1
                    p = tt % 2
                    eng.wait_ge(c_ge, tt + 1)
                    if tt >= 2:
                        eng.wait_ge(c_gttrT, tt - 1)
                    a_t = asb[tt % NASB]
                    eng.scalar_tensor_tensor(
                        gttm[p][:, :].rearrange("p (e r) -> p e r", e=8),
                        a_t[:, 200:328].rearrange("p (e r) -> p e r", e=8),
                        grec[p][:, 0:1],
                        gexp[p][:, :].unsqueeze(2).broadcast_to((128, 8, 16)),
                        ALU.mult, ALU.mult).then_inc(c_gt, 1)
                if t >= 1:
                    tt = t - 1
                    p = tt % 2
                    if tt >= 2:
                        eng.wait_ge(c_packB, tt - 1)
                    eng.wait_ge(c_oi, tt + 1)
                    eng.tensor_tensor(scext[p][0:64, :], o_ps, rpat[:],
                                      ALU.mult).then_inc(c_rs, 1)
                    if tt >= 2:
                        eng.wait_ge(c_packB, tt - 1)
                    eng.wait_ge(c_gttrT, tt + 1)
                    eng.tensor_copy(gtcm[p][:], gttr_ps).then_inc(c_gtev, 1)
                    eng.tensor_tensor(sttmp[:], st[:], dc128m[:], ALU.mult)
                    eng.wait_ge(c_Gt, tt + 1)
                    eng.tensor_tensor(st[:], sttmp[:], gt_ps, ALU.add)
                    eng.tensor_tensor(st64[:], st[:], dc64m[:], ALU.mult)
                    eng.tensor_copy(stbf[:], st[:])
                    eng.tensor_copy(st64bf[:], st64[:]).then_inc(c_st, 1)
                u = t + 4
                if 8 <= u < NT:
                    do_xpxn(u)
            eng.wait_ge(m_out, 16 * (NT - 1 - NOUT + 1))
            eng.wait_ge(c_packB, NT)
            eng.tensor_tensor(outb[(NT - 1) % NOUT][:], psB[(NT - 1) % 3][:],
                              xpxn[(NT - 1) % NPX][:], ALU.add)
            eng.drain(fusable=True).then_inc(c_Bev, 1)

        @block.scalar
        def _(eng):

            def stats(u):
                n = min(u + 2, NT)
                if n <= NXB:
                    eng.wait_ge(m_x0, 16 * n)
                else:
                    eng.wait_ge(m_x, 16 * (n - NXB))
                eng.activation(sqs[:], xb[u % NXB][:], AF.Square,
                               accum_out=s_sq[:, u:u + 1])
                eng.activation(cps[:], xb[u % NXB][:], AF.Identity,
                               accum_out=s_sum[:, u:u + 1])

            def finalizeS1(s):
                cs = slice(SEG * s, SEG * (s + 1))
                eng.activation(s_mean[:, cs], s_sum[:, cs], AF.Identity,
                               scale=1.0 / D)
                eng.activation(s_msq[:, cs], s_mean[:, cs], AF.Square)
                eng.activation(s_vq[:, cs], s_sq[:, cs], AF.Identity,
                               scale=1.0 / D)
                eng.drain(fusable=True).then_inc(c_sf, 1)

            def finalizeS2(s):
                cs = slice(SEG * s, SEG * (s + 1))
                eng.wait_ge(c_fv, 2 * s + 1)
                eng.activation(s_vq[:, cs], s_var[:, cs], AF.Ln,
                               bias=cst[:, 0:1])
                eng.activation(s_rstd[:, cs], s_vq[:, cs], AF.Exp,
                               scale=-0.5)
                eng.drain(fusable=True).then_inc(c_sf, 1)


            def do_xn(u):
                eng.wait_ge(c_fv, 2 * seg_of(u) + 2)
                if u >= NXN:
                    eng.wait_ge(m_xnst, 16 * (u - NXN + 1))
                if 8 <= u < 12:
                    eng.wait_ge(c_pt0, u - 7)
                eng.activation(xnbf[u % NXN][:], xb[u % NXB][:], AF.Identity,
                               bias=s_nmr[:, u:u + 1],
                               scale=s_rstd[:, u:u + 1]).then_inc(c_xnb2, 1)
                if u % SEG == SEG - 1:
                    eng.drain(fusable=True).then_inc(c_xnb, SEG)

            stats(0)
            stats(1)
            stats(2)
            stats(3)
            finalizeS1(0)
            finalizeS2(0)
            for u in range(4):
                do_xn(u)
            eng.wait_ge(m_wl2, 16 * NCONST)
            eng.activation(dsig[:], dec[:], AF.Exp, scale=-1.0)
            eng.activation(lnd[:], dsig[:], AF.Ln, bias=cst[0:1, 1:2])
            eng.drain(fusable=True).then_inc(c_sln, 1)
            eng.wait_ge(c_prepV, 1)
            eng.activation(lnd64[:], lnd[:], AF.Identity, scale=-64.0)
            eng.activation(lnd128[:], lnd[:], AF.Identity,
                           scale=-128.0)
            eng.drain(fusable=True).then_inc(c_sln2, 1)
            stats(4)
            stats(5)
            stats(6)
            stats(7)
            finalizeS1(1)
            finalizeS2(1)
            for u in range(4, 7):
                do_xn(u)
            stats(8)
            stats(9)
            stats(10)
            eng.wait_ge(c_pmm, 1)
            eng.activation(ptri[:], ps6[:, 0:64], AF.Exp)
            eng.activation(poff[:], ps6[0:64, 64:128], AF.Exp)
            eng.activation(pdp[:], ps6[:, 128:192], AF.Exp)
            eng.activation(rpat[:], ps6[0:64, 192:320], AF.Exp)
            eng.activation(dc64m[:], ps6[0:64, 320:384], AF.Exp)
            eng.activation(dc128m[:], ps6[0:64, 384:448], AF.Exp)
            eng.wait_ge(m_wl2, 16 * NCONST)
            eng.activation(vf[:], vf_st[:], AF.Identity, scale=1.0 / 16.0)
            eng.drain(fusable=True).then_inc(c_prepS, 1)


            for t in range(NT + 1):
                if t == 0:
                    eng.wait_ge(c_packA, 1)
                    eng.activation(asb[0][:], psA[:], AF.Identity)
                    eng.drain(fusable=True).then_inc(c_Aev, 1)
                if t >= 1:
                    tt = t - 1
                    p = tt % 2
                    if tt >= 2:
                        eng.wait_ge(c_gt, tt - 1)
                    eng.activation(gexp[p][:], asb[tt % NASB][:, 192:200],
                                   AF.Exp, accum_out=gsum[p][:])
                    eng.activation(gln[p][:], gsum[p][:], AF.Ln)
                    eng.activation(grec[p][:], gln[p][:], AF.Exp, scale=-1.0)
                    eng.drain(fusable=True).then_inc(c_ge, 1)
                u = t + 11
                if u < NT:
                    stats(u)
                if t % SEG == 0:
                    s = t // SEG + 2
                    if 2 <= s < NSEG:
                        finalizeS1(s)
                        finalizeS2(s)
                u = t + 7
                if 7 <= u < NT:
                    do_xn(u)
                if 1 <= t < NT:
                    if t >= 3:
                        eng.wait_ge(c_Gt, t - 2)
                        eng.wait_ge(c_gt, t - 2)
                    eng.wait_ge(c_packA, t + 1)
                    eng.activation(asb[t % NASB][:], psA[:],
                                   AF.Identity).then_inc(c_Aev, 1)

        @block.tensor
        def _(eng):
            eng.wait_ge(m_wl2, 16 * NCONST)
            for u in range(4):
                eng.wait_ge(c_xnb2, min(u + 2, 4))
                if u >= 2:
                    eng.wait_ge(c_xnt0, u - 1)
                half = psB[2][:, 512 * (u % 2):512 * (u % 2) + 512] \
                    .bitcast(BF16)
                for c in range(8):
                    mm = eng.transpose(half[:, 128 * c:128 * (c + 1)],
                                       xnbf[u][:, 128 * c:128 * (c + 1)],
                                       ident[:])
                    if c == 7:
                        mm.then_inc(c_pt0, 1)
            eng.wait_ge(c_sln2, 1)
            eng.matmul(ps6[:, 0:64], gv[0][:], lnd[:], start=True, stop=True)
            eng.matmul(ps6[0:64, 64:128], gv[1][0:1, 0:64], lnd[:],
                       start=True, stop=True)
            eng.matmul(ps6[:, 128:192], gv[2][:], lnd[:], start=True,
                       stop=True)
            eng.matmul(ps6[0:64, 192:320], lnd[:], gv[3][:], start=True,
                       stop=True)
            eng.matmul(ps6[0:64, 320:384], onesf[0:1, 0:64], lnd64[:],
                       start=True, stop=True)
            eng.matmul(ps6[0:64, 384:448], onesf[0:1, 0:64], lnd128[:],
                       start=True, stop=True).then_inc(c_pmm, 1)

            for t in range(NT + 1):
                if t >= 1:
                    tt = t - 1
                    p = tt % 2
                    a = asb[tt % NASB]
                    if tt == 0:
                        eng.wait_ge(m_wl2, 16 * NCONST)
                        eng.wait_ge(c_prepS, 1)
                    eng.wait_ge(c_Aev, tt + 1)
                    eng.wait_ge(c_st, tt + 1)
                    eng.transpose(trq_ps, a[:, 0:64], ident[:])
                    eng.transpose(trk_ps, a[:, 64:128],
                                  ident[:]).then_inc(c_trqk, 1)
                if t < NT:
                    s = seg_of(t)
                    pq = t % 3
                    off = (t % SEG) * 128
                    if t == 0:
                        eng.wait_ge(m_wl2, 16 * NCONST)
                    if s == 0:
                        eng.wait_ge(c_xnt0, 4)
                    else:
                        eng.wait_ge(m_xnt, 16 * 8 * s)
                    if t >= 3:
                        eng.wait_ge(c_Bev, t - 2)
                    if t == 0:
                        eng.wait_ge(m_wl, 16 * 8)
                    for c in range(8):
                        lhs = xnt[s % 3][c][:, off:off + 128]
                        if c < 7 or t == 0:
                            eng.matmul(psB[pq][:, 0:512], lhs,
                                       wcat[c][:, 328:840], start=(c == 0),
                                       stop=False)
                            eng.matmul(psB[pq][:, 512:1024], lhs,
                                       wcat[c][:, 840:1352], start=(c == 0),
                                       stop=False)
                        eng.matmul(psA[:], lhs, wcat[c][:, 0:328],
                                   start=(c == 0), stop=False)
                        if c == 2 and t >= 1:
                            eng.wait_ge(c_qkev, t)
                            eng.matmul(at_ps, kcm[(t - 1) % 2][:],
                                       qcm[(t - 1) % 2][:], start=True,
                                       stop=True).then_inc(c_At, 1)
                    eng.matmul(psA[:], onesbf[:], biasbf[:],
                               start=False, stop=True).then_inc(c_packA, 1)
                if t == NT:
                    eng.wait_ge(c_qkev, t)
                    eng.matmul(at_ps, kcm[(t - 1) % 2][:],
                               qcm[(t - 1) % 2][:], start=True,
                               stop=True).then_inc(c_At, 1)
                if t >= 1:
                    tt = t - 1
                    p = tt % 2
                    a = asb[tt % NASB]
                    p3 = tt % 3
                    eng.wait_ge(c_scA, tt + 1)
                    eng.wait_ge(c_G, tt + 1)
                    eng.matmul(o_ps[:, 0:128], vtri[p][:], atri[p][:],
                               start=True, stop=False)
                    eng.matmul(o_ps[:, 64:128], voff[p][:], aoff[p][:],
                               start=False, stop=False)
                    eng.matmul(o_ps[:, 64:128], st64bf[:],
                               qcm[p][:, 64:128], start=False, stop=False)
                    eng.matmul(o_ps[:, 0:64], stbf[:], qcm[p][:, 0:64],
                               start=False,
                               stop=True).then_inc(c_oi, 1)
                    lhs7 = xnt[seg_of(min(t, NT - 1)) % 3][7][
                        :, (t % SEG) * 128:(t % SEG) * 128 + 128]
                    if 1 <= t < NT:
                        eng.matmul(psB[t % 3][:, 0:512], lhs7,
                                   wcat[7][:, 328:840], start=False,
                                   stop=False)
                        eng.matmul(psB[t % 3][:, 512:1024], lhs7,
                                   wcat[7][:, 840:1352], start=False,
                                   stop=False)
                    eng.wait_ge(c_rs, tt + 1)
                    eng.matmul(gt_ps, a[:, 64:128], vdt[p][:], start=True,
                               stop=True).then_inc(c_Gt, 1)
                    eng.wait_ge(c_gt, tt + 1)
                    eng.transpose(gttr_ps, gttm[p][:],
                                  ident[:]).then_inc(c_gttrT, 1)
                    eng.matmul(psB[p3][:, 0:512], scext[p][0:65, :],
                               wot[0:65, 0:512], start=False, stop=False)
                    eng.matmul(psB[p3][:, 512:1024], scext[p][0:65, :],
                               wot[0:65, 512:1024], start=False, stop=False)
                    eng.wait_ge(c_gtev, tt + 1)
                    eng.matmul(psB[p3][:, 0:512], gtcm[p][:], vf[:, 0:512],
                               start=False, stop=True)
                    eng.matmul(psB[p3][:, 512:1024], gtcm[p][:],
                               vf[:, 512:1024], start=False,
                               stop=True).then_inc(c_packB, 1)

    ctx.pop_all()
    return nc


def host_layout(inputs):
    f32 = np.float32
    Wq = np.asarray(inputs["L0_attn_Wq"], f32)
    Wk = np.asarray(inputs["L0_attn_Wk"], f32)
    Wv = np.asarray(inputs["L0_attn_Wv"], f32)
    gw = np.asarray(inputs["L0_ffn_gate_w"], f32)
    U = np.asarray(inputs["L0_ffn_lora_U"], f32).reshape(128, 1024)
    Wb = np.asarray(inputs["L0_ffn_weight"], f32)
    wcatT = np.ascontiguousarray(
        np.concatenate([Wq, Wk, Wv, gw, U, Wb], axis=0).T)
    woT = np.ascontiguousarray(np.concatenate(
        [np.asarray(inputs["L0_attn_Wo"], f32).T,
         np.asarray(inputs["L0_attn_bo"], f32)[None, :]], axis=0))
    vfT = np.ascontiguousarray(
        np.asarray(inputs["L0_ffn_lora_V"], f32).transpose(0, 2, 1)
        .reshape(128, 1024))
    biascat = np.zeros((1, 328), f32)
    biascat[0, 0:64] = np.asarray(inputs["L0_attn_bq"], f32)
    biascat[0, 64:128] = np.asarray(inputs["L0_attn_bk"], f32)
    biascat[0, 128:192] = np.asarray(inputs["L0_attn_bv"], f32)
    decay2 = np.ascontiguousarray(
        np.asarray(inputs["L0_attn_decay"], f32)[None, :])
    sg, ta = np.arange(128), np.arange(128)
    mtri = ((sg[:, None] // HH == ta[None, :] // HH)
            & (sg[:, None] <= ta[None, :])).astype(f32).astype(BF)
    ident = np.eye(128, dtype=f32).astype(BF)
    gvs = [
        (sg % HH).astype(f32)[None, :],
        -np.where(sg < HH, HH - sg, 0).astype(f32)[None, :],
        -(128 - sg).astype(f32)[None, :],
        (-(sg % HH)).astype(f32)[None, :],
    ]
    return wcatT, woT, vfT, biascat, decay2, mtri, ident, gvs


def kernel(**inputs):
    x = np.ascontiguousarray(np.asarray(inputs["L0_attn_x"], np.float32))
    wcatT, woT, vfT, biascat, decay2, mtri, ident, gvs = host_layout(inputs)
    nc = build_nc()
    in_maps = []
    for b in range(8):
        im = {"x": np.ascontiguousarray(x[b]), "wcatT": wcatT, "woT": woT,
              "vfT": vfT, "biascat": biascat, "decay2": decay2,
              "mtri": mtri, "ident": ident}
        for i in range(4):
            im[f"gv{i}"] = gvs[i]
        in_maps.append(im)
    res = run_bass_kernel_spmd(nc, in_maps, core_ids=list(range(8)))
    out = np.stack([np.asarray(r["out"], np.float32) for r in res.results])
    return out


# version 55
# speedup vs baseline: 1.0200x; 1.0126x over previous
import numpy as np
import ml_dtypes
import concourse.bass as bass
import concourse.mybir as mybir
from concourse.bass_utils import run_bass_kernel_spmd
from contextlib import ExitStack

F32 = mybir.dt.float32
BF16 = mybir.dt.bfloat16
AF = mybir.ActivationFunctionType
ALU = mybir.AluOpType
X = mybir.AxisListType.X
BF = ml_dtypes.bfloat16

T, D, DS = 4096, 1024, 64
C, HH = 128, 64
NT = T // C
SEG = 4
NSEG = NT // SEG
PACKW = 64 * 3 + 8 + 128 + 1024
NXB, NPX, NXN, NOUT, NASB = 13, 8, 8, 3, 4
EPS = 1e-5


def build_nc():
    nc = bass.Bass("TRN2", target_bir_lowering=False,
                   detect_race_conditions=False)
    dt_in = dict(kind="ExternalInput")
    x_d = nc.dram_tensor("x", [T, D], F32, **dt_in).ap()
    wcat_d = nc.dram_tensor("wcatT", [D, PACKW], F32, **dt_in).ap()
    wot_d = nc.dram_tensor("woT", [65, D], F32, **dt_in).ap()
    vf_d = nc.dram_tensor("vfT", [128, D], F32, **dt_in).ap()
    bias_d = nc.dram_tensor("biascat", [1, 328], F32, **dt_in).ap()
    dec_d = nc.dram_tensor("decay2", [1, 64], F32, **dt_in).ap()
    mtri_d = nc.dram_tensor("mtri", [128, 128], BF16, **dt_in).ap()
    id_d = nc.dram_tensor("ident", [128, 128], BF16, **dt_in).ap()
    g_d = [nc.dram_tensor(f"gv{i}", [1, 128], F32, **dt_in).ap()
           for i in range(4)]
    xns_d = nc.dram_tensor("xnscr", [T, D], BF16, kind="Internal").ap()
    out_d = nc.dram_tensor("out", [T, D], F32, kind="ExternalOutput").ap()

    ctx = ExitStack()
    sb = lambda n, s, d: ctx.enter_context(nc.sbuf_tensor(n, s, d))
    ps = lambda n, s, d: ctx.enter_context(nc.psum_tensor(n, s, d))
    sem = lambda n: ctx.enter_context(nc.semaphore(n))

    wcat = [sb(f"wcat{c}", [128, PACKW], BF16) for c in range(8)]
    wot = sb("wot", [65, D], BF16)
    vf_st = sb("vf_st", [128, D], F32)
    vf = sb("vf", [128, D], BF16)
    biasbf = sb("biasbf", [1, 328], BF16)
    mtri = sb("mtri_s", [128, 128], BF16)
    ident = sb("ident_s", [128, 128], BF16)
    gv = [sb(f"gv_s{i}", [1, 128], F32) for i in range(4)]
    onesf = sb("onesf", [1, 128], F32)
    onesbf = sb("onesbf", [1, 128], BF16)
    dec = sb("dec", [1, 64], F32)
    dsig = sb("dsig", [1, 64], F32)
    lnd = sb("lnd", [1, 64], F32)
    lnd64 = sb("lnd64", [1, 64], F32)
    lnd128 = sb("lnd128", [1, 64], F32)
    ptri = sb("ptri", [128, 64], F32)
    poff = sb("poff", [64, 64], F32)
    pdp = sb("pdp", [128, 64], F32)
    rpat = sb("rpat", [64, 128], F32)
    dc64m = sb("dc64m", [64, 64], F32)
    dc128m = sb("dc128m", [64, 64], F32)
    cst = sb("cst", [128, 4], F32)

    xb = [sb(f"xb{i}", [128, D], F32) for i in range(NXB)]
    xpxn = [sb(f"xpxn{i}", [128, D], F32) for i in range(NPX)]
    xnbf = [sb(f"xnbf{i}", [128, D], BF16) for i in range(NXN)]
    xntbig = [sb(f"xntb{s}", [128, 8 * SEG * 128], BF16) for s in range(3)]
    xnt = [[xntbig[s][:, SEG * 128 * c:SEG * 128 * (c + 1)] for c in range(8)]
           for s in range(3)]
    asb = [sb(f"asb{i}", [128, 328], BF16) for i in range(NASB)]
    outb = [sb(f"outb{i}", [128, D], F32) for i in range(NOUT)]
    sqs = sb("sqs", [128, D], BF16)
    cps = sb("cps", [128, D], BF16)

    s_sum = sb("s_sum", [128, NT], F32)
    s_sq = sb("s_sq", [128, NT], F32)
    s_mean = sb("s_mean", [128, NT], F32)
    s_msq = sb("s_msq", [128, NT], F32)
    s_vq = sb("s_vq", [128, NT], F32)
    s_var = sb("s_var", [128, NT], F32)
    s_rstd = sb("s_rstd", [128, NT], F32)
    s_nmr = sb("s_nmr", [128, NT], F32)
    s_opr = sb("s_opr", [128, NT], F32)

    qcm = [sb(f"qcm{i}", [64, 128], BF16) for i in range(2)]
    kcm = [sb(f"kcm{i}", [64, 128], BF16) for i in range(2)]
    atri = [sb(f"atri{i}", [128, 128], BF16) for i in range(2)]
    aoff = [sb(f"aoff{i}", [64, 64], BF16) for i in range(2)]
    vtri = [sb(f"vtri{i}", [128, 64], BF16) for i in range(2)]
    voff = [sb(f"voff{i}", [64, 64], BF16) for i in range(2)]
    vdt = [sb(f"vdt{i}", [128, 64], BF16) for i in range(2)]
    scext = [sb(f"scext{i}", [65, 128], BF16) for i in range(2)]
    st = sb("st", [64, 64], F32)
    sttmp = sb("sttmp", [64, 64], F32)
    st64 = sb("st64", [64, 64], F32)
    stbf = sb("stbf", [64, 64], BF16)
    st64bf = sb("st64bf", [64, 64], BF16)
    gexp = [sb(f"gexp{i}", [128, 8], F32) for i in range(2)]
    gln = [sb(f"gln{i}", [128, 1], F32) for i in range(2)]
    gsum = [sb(f"gsum{i}", [128, 1], F32) for i in range(2)]
    grec = [sb(f"grec{i}", [128, 1], F32) for i in range(2)]
    gttm = [sb(f"gttm{i}", [128, 128], BF16) for i in range(2)]
    gtcm = [sb(f"gtcm{i}", [128, 128], BF16) for i in range(2)]

    psA = ps("psA", [128, 328], F32)
    psB = [ps(f"psB{i}", [128, 1024], F32) for i in range(3)]
    ps6 = ps("ps6", [128, 512], F32)
    ps7 = ps6
    at_ps = ps6[:, 0:128]
    o_ps = ps6[0:64, 128:256]
    gt_ps = ps6[0:64, 256:320]
    trq_ps = ps6[0:64, 320:384].bitcast(BF16)
    trk_ps = ps6[0:64, 384:448].bitcast(BF16)
    gttr_ps = ps6[:, 448:512].bitcast(BF16)

    m_x, m_wl, m_wl2 = sem("m_x"), sem("m_wl"), sem("m_wl2")
    m_x0 = sem("m_x0")
    m_xnst, m_xnt, m_out = sem("m_xnst"), sem("m_xnt"), sem("m_out")
    m_xnt2 = sem("m_xnt2")
    c_sln, c_sln2 = sem("c_sln"), sem("c_sln2")
    c_prepV, c_pmm, c_prepS = sem("c_prepV"), sem("c_pmm"), sem("c_prepS")
    c_sf, c_fv = sem("c_sf"), sem("c_fv")
    c_xnb, c_xpxn = sem("c_xnb"), sem("c_xpxn")
    c_xnb2 = sem("c_xnb2")
    c_packA, c_packB = sem("c_packA"), sem("c_packB")
    c_Aev, c_trqk, c_qkev = sem("c_Aev"), sem("c_trqk"), sem("c_qkev")
    c_At, c_scA, c_G = sem("c_At"), sem("c_scA"), sem("c_G")
    c_oi, c_Gt, c_st = sem("c_oi"), sem("c_Gt"), sem("c_st")
    c_rs, c_ge = sem("c_rs"), sem("c_ge")
    c_gt, c_gttrT, c_gtev = sem("c_gt"), sem("c_gttrT"), sem("c_gtev")
    c_Bev = sem("c_Bev")
    c_pt0, c_xnt0 = sem("c_pt0"), sem("c_xnt0")

    NCONST = 10

    seg_of = lambda u: u // SEG

    with nc.Block() as block:

        @block.sync
        def _(eng):
            for u in range(NXB):
                eng.dma_start(out=xb[u][:],
                              in_=x_d[128 * u:128 * (u + 1), :]
                              ).then_inc(m_x0, 16)

            def store_xn(u):
                eng.wait_ge(c_xnb, u + 1)
                eng.dma_start(out=xns_d[128 * u:128 * (u + 1), :],
                              in_=xnbf[u % NXN][:]).then_inc(m_xnst, 16)

            def transposes(s, c0, c1):
                eng.wait_ge(m_xnst, 16 * (SEG * (s + 1)))
                if s >= 2:
                    eng.wait_ge(m_xnt, 16 * 8 * (s - 1))
                if s >= 3:
                    eng.wait_ge(c_packA, SEG * (s - 2))
                t0 = s * SEG * 128
                for c in range(c0, c1):
                    eng.dma_start(
                        out=xnt[s % 3][c],
                        in_=xns_d[t0:t0 + SEG * 128, 128 * c:128 * (c + 1)],
                        transpose=True).then_inc(m_xnt, 16)

            for u in range(7):
                store_xn(u)
            for t in range(NT + 1):
                u = t + 7
                if 7 <= u < NT:
                    store_xn(u)
                if t % SEG == 0:
                    s = t // SEG + 1
                    if 1 <= s < NSEG:
                        transposes(s, 0, 8)

        @block.gpsimd
        def _(eng):
            def ldx(u):
                eng.dma_start(out=xb[u % NXB][:],
                              in_=x_d[128 * u:128 * (u + 1), :]
                              ).then_inc(m_x, 16)
            for i in range(4):
                eng.dma_start(out=gv[i][:], in_=g_d[i]).then_inc(m_wl2, 16)
            eng.dma_start(out=dec[:], in_=dec_d).then_inc(m_wl2, 16)
            eng.dma_start(out=mtri[:], in_=mtri_d).then_inc(m_wl2, 16)
            eng.dma_start(out=ident[:], in_=id_d).then_inc(m_wl2, 16)
            eng.dma_start(out=wot[:], in_=wot_d).then_inc(m_wl2, 16)
            eng.dma_start(out=biasbf[:], in_=bias_d).then_inc(m_wl2, 16)
            eng.dma_start(out=vf_st[:], in_=vf_d).then_inc(m_wl2, 16)
            for c in range(8):
                eng.dma_start(out=wcat[c][:],
                              in_=wcat_d[128 * c:128 * (c + 1), :]
                              ).then_inc(m_wl, 16)
            for t in range(NT + 1):
                u = t + NXB
                if u < NT:
                    eng.wait_ge(c_xnb2, u - NXB + 1)
                    eng.wait_ge(c_xpxn, u - NXB + 1)
                    ldx(u)
                if t >= 1:
                    tt = t - 1
                    p = tt % 2
                    a = asb[tt % NASB]
                    if tt == 0:
                        eng.wait_ge(c_prepS, 1)
                    if tt >= 2:
                        eng.wait_ge(c_Gt, tt - 1)
                    eng.wait_ge(c_ge, tt + 1)
                    eng.tensor_tensor(vtri[p][:], a[:, 128:192], ptri[:],
                                      ALU.mult)
                    eng.tensor_tensor(voff[p][:], a[0:64, 128:192], poff[:],
                                      ALU.mult)
                    eng.tensor_tensor(vdt[p][:], a[:, 128:192], pdp[:],
                                      ALU.mult).then_inc(c_G, 1)
                if t >= 2:
                    ss = t - 2
                    eng.wait_ge(c_Bev, ss + 1)
                    eng.dma_start(out=out_d[128 * ss:128 * (ss + 1), :],
                                  in_=outb[ss % NOUT][:]).then_inc(m_out, 16)
            eng.wait_ge(c_Bev, NT)
            eng.dma_start(out=out_d[128 * (NT - 1):128 * NT, :],
                          in_=outb[(NT - 1) % NOUT][:]).then_inc(m_out, 16)
            eng.wait_ge(m_out, 16 * NT)

        @block.vector
        def _(eng):
            eng.memset(cst[:, 0:1], EPS)
            eng.memset(cst[:, 1:2], 1.0)
            eng.memset(cst[:, 2:3], -1.0)
            eng.memset(cst[:, 3:4], 0.75)
            eng.memset(onesf[:], 1.0)
            eng.memset(onesbf[:], 1.0)
            eng.memset(scext[0][64:65, :], 1.0)
            eng.memset(scext[1][64:65, :], 1.0)
            eng.memset(st[:], 0.0)
            eng.memset(st64[:], 0.0)
            eng.memset(stbf[:], 0.0)
            eng.memset(st64bf[:], 0.0).then_inc(c_st, 1)

            def finalizeV(s):
                cs = slice(SEG * s, SEG * (s + 1))
                eng.wait_ge(c_sf, 2 * s + 1)
                eng.tensor_sub(s_var[:, cs], s_vq[:, cs],
                               s_msq[:, cs])
                eng.drain(fusable=True).then_inc(c_fv, 1)
                eng.wait_ge(c_sf, 2 * s + 2)
                eng.scalar_tensor_tensor(
                    s_nmr[:, cs], s_mean[:, cs], cst[:, 2:3], s_rstd[:, cs],
                    ALU.mult, ALU.mult)
                eng.tensor_scalar_add(s_opr[:, cs], s_rstd[:, cs],
                                      cst[:, 1:2])
                eng.drain(fusable=True).then_inc(c_fv, 1)

            def do_xpxn(u):
                if u + 1 <= NXB:
                    eng.wait_ge(m_x0, 16 * (u + 1))
                else:
                    eng.wait_ge(m_x, 16 * (u + 1 - NXB))
                eng.tensor_scalar(
                    xpxn[u % NPX][:], xb[u % NXB][:],
                    s_opr[:, u:u + 1], s_nmr[:, u:u + 1],
                    ALU.mult, ALU.add).then_inc(c_xpxn, 1)

            finalizeV(0)
            for u in range(4):
                eng.wait_ge(c_pt0, u + 1)
                src = psB[2][:, 512 * (u % 2):512 * (u % 2) + 512] \
                    .bitcast(BF16).rearrange("p (c r) -> p c r", r=128)
                dst = xntbig[0][:, :].rearrange(
                    "p (c r) -> p c r", r=SEG * 128)[:, :, 128 * u:128 * (u + 1)]
                if u < 3:
                    eng.tensor_copy(dst, src).then_inc(c_xnt0, 1)
                else:
                    eng.tensor_copy(dst, src)
                    eng.drain(fusable=True).then_inc(c_xnt0, 1)
            eng.wait_ge(c_sln, 1)
            eng.tensor_scalar_min(lnd[:], lnd[:], cst[0:1, 3:4])
            eng.drain(fusable=True).then_inc(c_prepV, 1)
            finalizeV(1)
            for u in range(8):
                do_xpxn(u)

            for t in range(NT + 1):
                if t >= 2:
                    ee = t - 2
                    if ee >= NOUT:
                        eng.wait_ge(m_out, 16 * (ee - NOUT + 1))
                    eng.wait_ge(c_packB, ee + 1)
                    eng.tensor_tensor(outb[ee % NOUT][:], psB[ee % 3][:],
                                      xpxn[ee % NPX][:],
                                      ALU.add)
                    eng.drain(fusable=True).then_inc(c_Bev, 1)
                if t >= 1:
                    tt = t - 1
                    p = tt % 2
                    if tt == 0:
                        eng.wait_ge(m_wl2, 16 * NCONST)
                        eng.wait_ge(c_prepS, 1)
                    if tt >= 2:
                        eng.wait_ge(c_oi, tt - 1)
                    eng.wait_ge(c_trqk, tt + 1)
                    eng.tensor_copy(qcm[p][:], trq_ps)
                    eng.tensor_copy(kcm[p][:], trk_ps).then_inc(c_qkev, 1)
                    if tt >= 2:
                        eng.wait_ge(c_Gt, tt - 1)
                    eng.wait_ge(c_At, tt + 1)
                    eng.tensor_tensor(atri[p][:], at_ps, mtri[:], ALU.mult)
                    eng.tensor_copy(aoff[p][:],
                                    at_ps[0:64, 64:128]).then_inc(c_scA, 1)
                if t % SEG == 0:
                    s = t // SEG + 2
                    if 2 <= s < NSEG:
                        finalizeV(s)
                if t >= 1:
                    tt = t - 1
                    p = tt % 2
                    eng.wait_ge(c_ge, tt + 1)
                    if tt >= 2:
                        eng.wait_ge(c_gttrT, tt - 1)
                    a_t = asb[tt % NASB]
                    eng.scalar_tensor_tensor(
                        gttm[p][:, :].rearrange("p (e r) -> p e r", e=8),
                        a_t[:, 200:328].rearrange("p (e r) -> p e r", e=8),
                        grec[p][:, 0:1],
                        gexp[p][:, :].unsqueeze(2).broadcast_to((128, 8, 16)),
                        ALU.mult, ALU.mult).then_inc(c_gt, 1)
                if t >= 1:
                    tt = t - 1
                    p = tt % 2
                    if tt >= 2:
                        eng.wait_ge(c_packB, tt - 1)
                    eng.wait_ge(c_oi, tt + 1)
                    eng.tensor_tensor(scext[p][0:64, :], o_ps, rpat[:],
                                      ALU.mult).then_inc(c_rs, 1)
                    if tt >= 2:
                        eng.wait_ge(c_packB, tt - 1)
                    eng.wait_ge(c_gttrT, tt + 1)
                    eng.tensor_copy(gtcm[p][:], gttr_ps).then_inc(c_gtev, 1)
                    eng.tensor_tensor(sttmp[:], st[:], dc128m[:], ALU.mult)
                    eng.wait_ge(c_Gt, tt + 1)
                    eng.tensor_tensor(st[:], sttmp[:], gt_ps, ALU.add)
                    eng.tensor_tensor(st64[:], st[:], dc64m[:], ALU.mult)
                    eng.tensor_copy(stbf[:], st[:])
                    eng.tensor_copy(st64bf[:], st64[:]).then_inc(c_st, 1)
                u = t + 4
                if 8 <= u < NT:
                    do_xpxn(u)
            eng.wait_ge(m_out, 16 * (NT - 1 - NOUT + 1))
            eng.wait_ge(c_packB, NT)
            eng.tensor_tensor(outb[(NT - 1) % NOUT][:], psB[(NT - 1) % 3][:],
                              xpxn[(NT - 1) % NPX][:], ALU.add)
            eng.drain(fusable=True).then_inc(c_Bev, 1)

        @block.scalar
        def _(eng):

            def stats(u):
                n = min(u + 2, NT)
                if n <= NXB:
                    eng.wait_ge(m_x0, 16 * n)
                else:
                    eng.wait_ge(m_x, 16 * (n - NXB))
                eng.activation(sqs[:], xb[u % NXB][:], AF.Square,
                               accum_out=s_sq[:, u:u + 1])
                eng.activation(cps[:], xb[u % NXB][:], AF.Identity,
                               accum_out=s_sum[:, u:u + 1])

            def finalizeS1(s):
                cs = slice(SEG * s, SEG * (s + 1))
                eng.activation(s_mean[:, cs], s_sum[:, cs], AF.Identity,
                               scale=1.0 / D)
                eng.activation(s_msq[:, cs], s_mean[:, cs], AF.Square)
                eng.activation(s_vq[:, cs], s_sq[:, cs], AF.Identity,
                               scale=1.0 / D)
                eng.drain(fusable=True).then_inc(c_sf, 1)

            def finalizeS2(s):
                cs = slice(SEG * s, SEG * (s + 1))
                eng.wait_ge(c_fv, 2 * s + 1)
                eng.activation(s_vq[:, cs], s_var[:, cs], AF.Ln,
                               bias=cst[:, 0:1])
                eng.activation(s_rstd[:, cs], s_vq[:, cs], AF.Exp,
                               scale=-0.5)
                eng.drain(fusable=True).then_inc(c_sf, 1)


            def do_xn(u):
                eng.wait_ge(c_fv, 2 * seg_of(u) + 2)
                if u >= NXN:
                    eng.wait_ge(m_xnst, 16 * (u - NXN + 1))
                if 8 <= u < 12:
                    eng.wait_ge(c_pt0, u - 7)
                eng.activation(xnbf[u % NXN][:], xb[u % NXB][:], AF.Identity,
                               bias=s_nmr[:, u:u + 1],
                               scale=s_rstd[:, u:u + 1]).then_inc(c_xnb2, 1)
                if u % SEG == SEG - 1:
                    eng.drain(fusable=True).then_inc(c_xnb, SEG)

            stats(0)
            stats(1)
            stats(2)
            stats(3)
            finalizeS1(0)
            finalizeS2(0)
            for u in range(4):
                do_xn(u)
            eng.wait_ge(m_wl2, 16 * NCONST)
            eng.activation(dsig[:], dec[:], AF.Exp, scale=-1.0)
            eng.activation(lnd[:], dsig[:], AF.Ln, bias=cst[0:1, 1:2])
            eng.drain(fusable=True).then_inc(c_sln, 1)
            eng.wait_ge(c_prepV, 1)
            eng.activation(lnd64[:], lnd[:], AF.Identity, scale=-64.0)
            eng.activation(lnd128[:], lnd[:], AF.Identity,
                           scale=-128.0)
            eng.drain(fusable=True).then_inc(c_sln2, 1)
            stats(4)
            stats(5)
            stats(6)
            stats(7)
            finalizeS1(1)
            finalizeS2(1)
            for u in range(4, 7):
                do_xn(u)
            stats(8)
            stats(9)
            stats(10)
            eng.wait_ge(c_pmm, 1)
            eng.activation(ptri[:], ps6[:, 0:64], AF.Exp)
            eng.activation(poff[:], ps6[0:64, 64:128], AF.Exp)
            eng.activation(pdp[:], ps6[:, 128:192], AF.Exp)
            eng.activation(rpat[:], ps6[0:64, 192:320], AF.Exp)
            eng.activation(dc64m[:], ps6[0:64, 320:384], AF.Exp)
            eng.activation(dc128m[:], ps6[0:64, 384:448], AF.Exp)
            eng.wait_ge(m_wl2, 16 * NCONST)
            eng.activation(vf[:], vf_st[:], AF.Identity, scale=1.0 / 16.0)
            eng.drain(fusable=True).then_inc(c_prepS, 1)


            for t in range(NT + 1):
                if t == 0:
                    eng.wait_ge(c_packA, 1)
                    eng.activation(asb[0][:], psA[:], AF.Identity)
                    eng.drain(fusable=True).then_inc(c_Aev, 1)
                if t >= 1:
                    tt = t - 1
                    p = tt % 2
                    if tt >= 2:
                        eng.wait_ge(c_gt, tt - 1)
                    eng.activation(gexp[p][:], asb[tt % NASB][:, 192:200],
                                   AF.Exp, accum_out=gsum[p][:])
                    eng.activation(gln[p][:], gsum[p][:], AF.Ln)
                    eng.activation(grec[p][:], gln[p][:], AF.Exp, scale=-1.0)
                    eng.drain(fusable=True).then_inc(c_ge, 1)
                u = t + 11
                if u < NT:
                    stats(u)
                if t % SEG == 0:
                    s = t // SEG + 2
                    if 2 <= s < NSEG:
                        finalizeS1(s)
                        finalizeS2(s)
                if 1 <= t < NT:
                    if t >= 3:
                        eng.wait_ge(c_Gt, t - 2)
                        eng.wait_ge(c_gt, t - 2)
                    eng.wait_ge(c_packA, t + 1)
                    eng.activation(asb[t % NASB][:], psA[:],
                                   AF.Identity).then_inc(c_Aev, 1)
                u = t + 7
                if 7 <= u < NT:
                    do_xn(u)

        @block.tensor
        def _(eng):
            eng.wait_ge(m_wl2, 16 * NCONST)
            for u in range(4):
                eng.wait_ge(c_xnb2, min(u + 2, 4))
                if u >= 2:
                    eng.wait_ge(c_xnt0, u - 1)
                half = psB[2][:, 512 * (u % 2):512 * (u % 2) + 512] \
                    .bitcast(BF16)
                for c in range(8):
                    mm = eng.transpose(half[:, 128 * c:128 * (c + 1)],
                                       xnbf[u][:, 128 * c:128 * (c + 1)],
                                       ident[:])
                    if c == 7:
                        mm.then_inc(c_pt0, 1)
            eng.wait_ge(c_sln2, 1)
            eng.matmul(ps6[:, 0:64], gv[0][:], lnd[:], start=True, stop=True)
            eng.matmul(ps6[0:64, 64:128], gv[1][0:1, 0:64], lnd[:],
                       start=True, stop=True)
            eng.matmul(ps6[:, 128:192], gv[2][:], lnd[:], start=True,
                       stop=True)
            eng.matmul(ps6[0:64, 192:320], lnd[:], gv[3][:], start=True,
                       stop=True)
            eng.matmul(ps6[0:64, 320:384], onesf[0:1, 0:64], lnd64[:],
                       start=True, stop=True)
            eng.matmul(ps6[0:64, 384:448], onesf[0:1, 0:64], lnd128[:],
                       start=True, stop=True).then_inc(c_pmm, 1)

            for t in range(NT + 1):
                if t >= 1:
                    tt = t - 1
                    p = tt % 2
                    a = asb[tt % NASB]
                    if tt == 0:
                        eng.wait_ge(m_wl2, 16 * NCONST)
                        eng.wait_ge(c_prepS, 1)
                    eng.wait_ge(c_Aev, tt + 1)
                    eng.wait_ge(c_st, tt + 1)
                    eng.transpose(trq_ps, a[:, 0:64], ident[:])
                    eng.transpose(trk_ps, a[:, 64:128],
                                  ident[:]).then_inc(c_trqk, 1)
                if t < NT:
                    s = seg_of(t)
                    pq = t % 3
                    off = (t % SEG) * 128
                    if t == 0:
                        eng.wait_ge(m_wl2, 16 * NCONST)
                    if s == 0:
                        eng.wait_ge(c_xnt0, 4)
                    else:
                        eng.wait_ge(m_xnt, 16 * 8 * s)
                    if t >= 3:
                        eng.wait_ge(c_Bev, t - 2)
                    if t == 0:
                        eng.wait_ge(m_wl, 16 * 8)
                    for c in range(8):
                        lhs = xnt[s % 3][c][:, off:off + 128]
                        if c < 7 or t == 0:
                            eng.matmul(psB[pq][:, 0:512], lhs,
                                       wcat[c][:, 328:840], start=(c == 0),
                                       stop=False)
                            eng.matmul(psB[pq][:, 512:1024], lhs,
                                       wcat[c][:, 840:1352], start=(c == 0),
                                       stop=False)
                        eng.matmul(psA[:], lhs, wcat[c][:, 0:328],
                                   start=(c == 0), stop=False)
                        if c == 2 and t >= 1:
                            eng.wait_ge(c_qkev, t)
                            eng.matmul(at_ps, kcm[(t - 1) % 2][:],
                                       qcm[(t - 1) % 2][:], start=True,
                                       stop=True).then_inc(c_At, 1)
                    eng.matmul(psA[:], onesbf[:], biasbf[:],
                               start=False, stop=True).then_inc(c_packA, 1)
                if t == NT:
                    eng.wait_ge(c_qkev, t)
                    eng.matmul(at_ps, kcm[(t - 1) % 2][:],
                               qcm[(t - 1) % 2][:], start=True,
                               stop=True).then_inc(c_At, 1)
                if t >= 1:
                    tt = t - 1
                    p = tt % 2
                    a = asb[tt % NASB]
                    p3 = tt % 3
                    eng.wait_ge(c_scA, tt + 1)
                    eng.wait_ge(c_G, tt + 1)
                    eng.matmul(o_ps[:, 0:128], vtri[p][:], atri[p][:],
                               start=True, stop=False)
                    eng.matmul(o_ps[:, 64:128], voff[p][:], aoff[p][:],
                               start=False, stop=False)
                    eng.matmul(o_ps[:, 64:128], st64bf[:],
                               qcm[p][:, 64:128], start=False, stop=False)
                    eng.matmul(o_ps[:, 0:64], stbf[:], qcm[p][:, 0:64],
                               start=False,
                               stop=True).then_inc(c_oi, 1)
                    lhs7 = xnt[seg_of(min(t, NT - 1)) % 3][7][
                        :, (t % SEG) * 128:(t % SEG) * 128 + 128]
                    if 1 <= t < NT:
                        eng.matmul(psB[t % 3][:, 0:512], lhs7,
                                   wcat[7][:, 328:840], start=False,
                                   stop=False)
                        eng.matmul(psB[t % 3][:, 512:1024], lhs7,
                                   wcat[7][:, 840:1352], start=False,
                                   stop=False)
                    eng.wait_ge(c_rs, tt + 1)
                    eng.matmul(gt_ps, a[:, 64:128], vdt[p][:], start=True,
                               stop=True).then_inc(c_Gt, 1)
                    eng.wait_ge(c_gt, tt + 1)
                    eng.transpose(gttr_ps, gttm[p][:],
                                  ident[:]).then_inc(c_gttrT, 1)
                    eng.matmul(psB[p3][:, 0:512], scext[p][0:65, :],
                               wot[0:65, 0:512], start=False, stop=False)
                    eng.matmul(psB[p3][:, 512:1024], scext[p][0:65, :],
                               wot[0:65, 512:1024], start=False, stop=False)
                    eng.wait_ge(c_gtev, tt + 1)
                    eng.matmul(psB[p3][:, 0:512], gtcm[p][:], vf[:, 0:512],
                               start=False, stop=True)
                    eng.matmul(psB[p3][:, 512:1024], gtcm[p][:],
                               vf[:, 512:1024], start=False,
                               stop=True).then_inc(c_packB, 1)

    ctx.pop_all()
    return nc


def host_layout(inputs):
    f32 = np.float32
    Wq = np.asarray(inputs["L0_attn_Wq"], f32)
    Wk = np.asarray(inputs["L0_attn_Wk"], f32)
    Wv = np.asarray(inputs["L0_attn_Wv"], f32)
    gw = np.asarray(inputs["L0_ffn_gate_w"], f32)
    U = np.asarray(inputs["L0_ffn_lora_U"], f32).reshape(128, 1024)
    Wb = np.asarray(inputs["L0_ffn_weight"], f32)
    wcatT = np.ascontiguousarray(
        np.concatenate([Wq, Wk, Wv, gw, U, Wb], axis=0).T)
    woT = np.ascontiguousarray(np.concatenate(
        [np.asarray(inputs["L0_attn_Wo"], f32).T,
         np.asarray(inputs["L0_attn_bo"], f32)[None, :]], axis=0))
    vfT = np.ascontiguousarray(
        np.asarray(inputs["L0_ffn_lora_V"], f32).transpose(0, 2, 1)
        .reshape(128, 1024))
    biascat = np.zeros((1, 328), f32)
    biascat[0, 0:64] = np.asarray(inputs["L0_attn_bq"], f32)
    biascat[0, 64:128] = np.asarray(inputs["L0_attn_bk"], f32)
    biascat[0, 128:192] = np.asarray(inputs["L0_attn_bv"], f32)
    decay2 = np.ascontiguousarray(
        np.asarray(inputs["L0_attn_decay"], f32)[None, :])
    sg, ta = np.arange(128), np.arange(128)
    mtri = ((sg[:, None] // HH == ta[None, :] // HH)
            & (sg[:, None] <= ta[None, :])).astype(f32).astype(BF)
    ident = np.eye(128, dtype=f32).astype(BF)
    gvs = [
        (sg % HH).astype(f32)[None, :],
        -np.where(sg < HH, HH - sg, 0).astype(f32)[None, :],
        -(128 - sg).astype(f32)[None, :],
        (-(sg % HH)).astype(f32)[None, :],
    ]
    return wcatT, woT, vfT, biascat, decay2, mtri, ident, gvs


def kernel(**inputs):
    x = np.ascontiguousarray(np.asarray(inputs["L0_attn_x"], np.float32))
    wcatT, woT, vfT, biascat, decay2, mtri, ident, gvs = host_layout(inputs)
    nc = build_nc()
    in_maps = []
    for b in range(8):
        im = {"x": np.ascontiguousarray(x[b]), "wcatT": wcatT, "woT": woT,
              "vfT": vfT, "biascat": biascat, "decay2": decay2,
              "mtri": mtri, "ident": ident}
        for i in range(4):
            im[f"gv{i}"] = gvs[i]
        in_maps.append(im)
    res = run_bass_kernel_spmd(nc, in_maps, core_ids=list(range(8)))
    out = np.stack([np.asarray(r["out"], np.float32) for r in res.results])
    return out
